# version 08 (final)
# speedup vs baseline: 1.2932x; 1.0096x over previous
import math
import contextlib
import numpy as np
import ml_dtypes
import concourse.bass as bass
import concourse.mybir as mybir
from concourse.bass_utils import run_bass_kernel_spmd

F32 = mybir.dt.float32
BF16 = mybir.dt.bfloat16
AX = mybir.AxisListType.X
ALU = mybir.AluOpType
ACT = mybir.ActivationFunctionType

N, D, NT, ND = 2048, 1024, 16, 8
HR, HO, HP, INNER = 8, 16, 8, 4096
M32 = 12582912.0
M64 = 6755399441055744.0
NCORE = 8

C_MASK4, C_3M, C_1023, C_12, C_ONE = 0, 1, 2, 3, 4
G_R3M, G_R1023, G_R12, G_MU, G_NEGMU, G_RSTD, G_WS, G_C = 0, 1, 2, 3, 4, 5, 6, 7


class Ch:
    def __init__(self, es, nc, name):
        self.h = es.enter_context(nc.semaphore(name))
        self.n = 0

    def inc(self, inst, k=1):
        inst.then_inc(self.h, k)
        self.n += k
        return self.n



class SerialBlock:

    def __init__(self, bld):
        self._bld = bld
        self._seen = {}

    def _wrap(self, name):
        def deco(fn):
            real = getattr(self._bld.block, name)
            seen = self._seen.setdefault(name, {})
            b = self._bld
            @real
            def _(eng):
                for ch in (b.ld, b.qg, b.t, b.v, b.s):
                    if seen.get(id(ch), -1) < ch.n:
                        eng.wait_ge(ch.h, ch.n)
                        seen[id(ch)] = ch.n
                fn(eng)
        return deco

    @property
    def sync(self):
        return self._wrap("sync")

    @property
    def gpsimd(self):
        return self._wrap("gpsimd")

    @property
    def tensor(self):
        return self._wrap("tensor")

    @property
    def vector(self):
        return self._wrap("vector")

    @property
    def scalar(self):
        return self._wrap("scalar")


class Builder:
    def __init__(self, tap=None, upto=99):
        self.tap = tap
        self.upto = upto
        nc = bass.Bass("TRN2", target_bir_lowering=False,
                       detect_race_conditions=False)
        self.nc = nc
        dr = lambda n, s, dt=F32, kind="ExternalInput": \
            nc.dram_tensor(n, s, dt, kind=kind).ap()
        self.x_d = dr("x", [N, D])
        self.w_in = {
            "wq": dr("wq", [D, D]), "wk": dr("wk", [D, D]),
            "wv": dr("wv", [D, D]), "wg": dr("wg", [D, D]),
            "wor": dr("wor", [D, D]), "wqkv": dr("wqkv", [3 * D, D]),
            "woo": dr("woo", [D, D]), "wgate": dr("wgate", [INNER, D]),
            "wup": dr("wup", [INNER, D]), "wdown": dr("wdown", [D, INNER]),
        }
        self.prm_d = dr("prm", [1, 4 * D])
        self.idn_d = dr("idn", [128, 128], BF16)
        self.cmask_d = dr("cmask", [128, 128], BF16)
        self.gpvh_d = dr("gpvh", [128, 128]); self.gpvl_d = dr("gpvl", [128, 128])
        self.gpyh_d = dr("gpyh", [128, 128]); self.gpyl_d = dr("gpyl", [128, 128])
        self.csts_d = dr("csts", [128, 8])
        self.onesr_d = dr("onesr", [1, 128])
        self.hones_d = dr("hones", [128, 2])
        self.out_d = dr("out", [N, D], F32, "ExternalOutput")
        if tap is not None:
            self.tap_d = dr("tapo", [N, D], F32, "ExternalOutput")
        ds = lambda n, s, dt=F32: nc.dram_tensor(n, s, dt, kind="Internal").ap()
        self.xn_h = ds("xn_h", [N, D])
        self.g_h = ds("g_h", [N, D])
        self.z_h = [ds("z_h0", [N, D]), ds("z_h1", [N, D])]
        self.k_h = [ds(f"k_h{i}", [N, D], BF16) for i in range(3)]
        self.wT = {
            "wq": ds("wTq", [D, D], BF16), "wk": ds("wTk", [D, D], BF16),
            "wv": ds("wTv", [D, D], BF16), "wg": ds("wTg", [D, D], BF16),
            "wor": ds("wTor", [D, D], BF16),
            "wqkv": ds("wTqkv", [D, 3 * D], BF16),
            "woo": ds("wToo", [D, D], BF16),
            "wgate": ds("wTgate", [D, INNER], BF16),
            "wup": ds("wTup", [D, INNER], BF16),
            "wdown": ds("wTdown", [INNER, D], BF16),
        }

    def build(self):
        nc = self.nc
        with contextlib.ExitStack() as es:
            E = es.enter_context
            self.ld = Ch(es, nc, "ld")
            self.qg = Ch(es, nc, "qg")
            self.t = Ch(es, nc, "t")
            self.v = Ch(es, nc, "v")
            self.s = Ch(es, nc, "s")
            self.midn = E(nc.sbuf_tensor("midn", [128, 128], BF16))
            self.mcmask = E(nc.sbuf_tensor("mcmask", [128, 128], BF16))
            self.gpvh = E(nc.sbuf_tensor("s_gpvh", [128, 128], F32))
            self.gpvl = E(nc.sbuf_tensor("s_gpvl", [128, 128], F32))
            self.gpyh = E(nc.sbuf_tensor("s_gpyh", [128, 128], F32))
            self.gpyl = E(nc.sbuf_tensor("s_gpyl", [128, 128], F32))
            self.csts = E(nc.sbuf_tensor("s_csts", [128, 8], F32))
            self.onesr = E(nc.sbuf_tensor("s_onesr", [1, 128], F32))
            self.hones = E(nc.sbuf_tensor("s_hones", [128, 2], F32))
            self.m1 = E(nc.sbuf_tensor("m1", [128, 1], F32))
            self.prow = E(nc.sbuf_tensor("prow", [1, 4 * D], F32))
            self.aR = E(nc.sbuf_tensor("aR", [128, D], F32))
            self.bR = E(nc.sbuf_tensor("bR", [128, D], F32))
            self.aO = E(nc.sbuf_tensor("aO", [128, D], F32))
            self.bO = E(nc.sbuf_tensor("bO", [128, D], F32))
            self.smin = E(nc.sbuf_tensor("smin", [128, NT], F32))
            self.smax = E(nc.sbuf_tensor("smax", [128, NT], F32))
            self.ssum = E(nc.sbuf_tensor("ssum", [128, NT], F32))
            self.ssq = E(nc.sbuf_tensor("ssq", [128, NT], F32))
            self.stA = E(nc.sbuf_tensor("stA", [128, NT], F32))
            self.stB = E(nc.sbuf_tensor("stB", [128, NT], F32))
            self.stR = E(nc.sbuf_tensor("stR", [128, NT], F32))
            self.st0 = E(nc.sbuf_tensor("st0", [128, NT], F32))
            self.st1 = E(nc.sbuf_tensor("st1", [128, NT], F32))
            self.gco = E(nc.sbuf_tensor("gco", [128, 16], F32))
            self.t11 = E(nc.sbuf_tensor("t11", [1, 16], F32))
            self.negCb = E(nc.sbuf_tensor("negCb", [128, 2], F32))
            self.q_t1 = E(nc.sbuf_tensor("q_t1", [128, D], F32))
            self.q_z = E(nc.sbuf_tensor("q_z", [128, D], F32))
            self.q_u = E(nc.sbuf_tensor("q_u", [128, D], F32))
            self.q_r = E(nc.sbuf_tensor("q_r", [128, D], F32))
            self.q_xq = E(nc.sbuf_tensor("q_xq", [128, D], BF16))
            self.scr = E(nc.sbuf_tensor("scr", [128, 2048], F32))
            self.scrb = E(nc.sbuf_tensor("scrb", [128, 2048], BF16))
            self.xt = [E(nc.sbuf_tensor(f"xt{i}", [128, D], F32))
                       for i in range(2)]
            self.pA = E(nc.psum_tensor("pA", [128, 2048], F32))
            self.pC1 = E(nc.psum_tensor("pC1", [128, 512], F32))
            self.pC2 = E(nc.psum_tensor("pC2", [128, 512], F32))
            self.pD = E(nc.psum_tensor("pD", [128, 512], F32))
            self.pT = E(nc.psum_tensor("pT", [128, 1024], BF16))
            self.block = E(nc.Block())
            self.sblock = SerialBlock(self)

            self.preamble()
            if self.upto >= 1:
                self.weight_quant()
            if self.upto >= 2:
                self.p0_and_ffn()
            if self.upto >= 4:
                self.retention()
            if self.upto >= 5:
                self.ode_all(min(8, self.upto - 4))
            if self.upto >= 13:
                self.final_combine()
            self.emit_tap()
            self.barrier()
        return nc

    def vwait_all(self, eng):
        for ch in (self.ld, self.qg, self.t, self.v, self.s):
            eng.wait_ge(ch.h, ch.n)

    def barrier(self):
        b = self.sblock
        @b.sync
        def _(e): self.vwait_all(e)
        @b.gpsimd
        def _(e): self.vwait_all(e)
        @b.tensor
        def _(e): self.vwait_all(e)
        @b.vector
        def _(e): self.vwait_all(e)
        @b.scalar
        def _(e): self.vwait_all(e)

    def vs(self, ve, inst):
        self.v.inc(inst)
        ve.wait_ge(self.v.h, self.v.n)

    def transpose_to(self, src_bf, nblk, out_ap_of_batch):
        b = self.sblock
        nbat = (nblk + 7) // 8
        for q in range(nbat):
            nb = min(8, nblk - q * 8)
            @b.tensor
            def _(te, q=q, nb=nb):
                te.wait_ge(self.s.h, self.s.n)
                te.wait_ge(self.v.h, self.v.n)
                for j in range(nb):
                    self.t.inc(te.transpose(
                        self.pT[:, j * 128:(j + 1) * 128],
                        src_bf[:, (q * 8 + j) * 128:(q * 8 + j + 1) * 128],
                        self.midn[:]))
            @b.vector
            def _(ve, q=q, nb=nb):
                ve.wait_ge(self.t.h, self.t.n)
                self.v.inc(ve.tensor_copy(
                    out=out_ap_of_batch(q, nb),
                    in_=self.pT[:, 0:nb * 128].rearrange(
                        "p (a c) -> p a c", a=nb)))

    def bl_finalize(self, cnt_log2):
        b = self.sblock
        @b.vector
        def _(ve):
            self.vwait_all(ve)
            self.vs(ve, ve.reduce_sum(out=self.st0[:, 0:1],
                                      in_=self.ssum[:], axis=AX))
            self.vs(ve, ve.reduce_sum(out=self.st0[:, 1:2],
                                      in_=self.ssq[:], axis=AX))
        @b.tensor
        def _(te):
            te.wait_ge(self.v.h, self.v.n)
            self.t.inc(te.matmul(self.pD[0:1, 4:5], self.st0[:, 0:1],
                                 self.csts[:, C_ONE:C_ONE + 1],
                                 start=True, stop=True))
            self.t.inc(te.matmul(self.pD[0:1, 5:6], self.st0[:, 1:2],
                                 self.csts[:, C_ONE:C_ONE + 1],
                                 start=True, stop=True))
        @b.vector
        def _(ve):
            ve.wait_ge(self.t.h, self.t.n)
            self.vs(ve, ve.tensor_copy(out=self.t11[:, 0:2],
                                       in_=self.pD[0:1, 4:6]))
            sc = 2.0 ** -cnt_log2
            self.vs(ve, ve.tensor_scalar_mul(out=self.t11[:, 2:3],
                                             in0=self.t11[:, 0:1], scalar1=sc))
            self.vs(ve, ve.tensor_scalar_mul(out=self.t11[:, 3:4],
                                             in0=self.t11[:, 1:2], scalar1=sc))
            self.vs(ve, ve.tensor_mul(out=self.t11[:, 4:5],
                                      in0=self.t11[:, 2:3],
                                      in1=self.t11[:, 2:3]))
            self.vs(ve, ve.tensor_sub(out=self.t11[:, 5:6],
                                      in0=self.t11[:, 3:4],
                                      in1=self.t11[:, 4:5]))
            self.vs(ve, ve.tensor_scalar_add(out=self.t11[:, 6:7],
                                             in0=self.t11[:, 5:6],
                                             scalar1=1e-5))
            self.vs(ve, ve.reciprocal(out=self.t11[:, 7:8],
                                      in_=self.t11[:, 6:7]))
        @b.scalar
        def _(se):
            se.wait_ge(self.v.h, self.v.n)
            self.s.inc(se.activation(out=self.t11[:, 8:9],
                                     in_=self.t11[:, 7:8], func=ACT.Sqrt))
        @b.vector
        def _(ve):
            ve.wait_ge(self.s.h, self.s.n)
            self.vs(ve, ve.tensor_scalar_mul(out=self.t11[:, 9:10],
                                             in0=self.t11[:, 2:3],
                                             scalar1=-1.0))
        @b.tensor
        def _(te):
            te.wait_ge(self.v.h, self.v.n)
            self.t.inc(te.matmul(self.pD[:, 0:1], self.onesr[:],
                                 self.t11[:, 2:3], start=True, stop=True))
            self.t.inc(te.matmul(self.pD[:, 1:2], self.onesr[:],
                                 self.t11[:, 9:10], start=True, stop=True))
            self.t.inc(te.matmul(self.pD[:, 2:3], self.onesr[:],
                                 self.t11[:, 8:9], start=True, stop=True))
        @b.vector
        def _(ve):
            ve.wait_ge(self.t.h, self.t.n)
            self.vs(ve, ve.tensor_copy(out=self.gco[:, G_MU:G_MU + 3],
                                       in_=self.pD[:, 0:3]))
            self.vs(ve, ve.tensor_scalar(
                out=self.st0[:], in0=self.smax[:], scalar1=1.0,
                scalar2=self.gco[:, G_NEGMU:G_NEGMU + 1],
                op0=ALU.mult, op1=ALU.add))
            self.vs(ve, ve.tensor_scalar(
                out=self.st1[:], in0=self.smin[:], scalar1=-1.0,
                scalar2=self.gco[:, G_MU:G_MU + 1],
                op0=ALU.mult, op1=ALU.add))
            self.vs(ve, ve.tensor_max(out=self.st0[:], in0=self.st0[:],
                                      in1=self.st1[:]))
            self.vs(ve, ve.tensor_scalar_mul(
                out=self.st1[:], in0=self.st0[:],
                scalar1=self.gco[:, G_RSTD:G_RSTD + 1]))
            self.vs(ve, ve.tensor_scalar_max(out=self.st0[:], in0=self.st1[:],
                                             scalar1=1e-5))
            self.vs(ve, ve.reciprocal(out=self.st1[:], in_=self.st0[:]))
            self.vs(ve, ve.tensor_scalar_mul(out=self.st0[:], in0=self.st1[:],
                                             scalar1=127.0))
            self.vs(ve, ve.tensor_scalar_mul(
                out=self.stA[:], in0=self.st0[:],
                scalar1=self.gco[:, G_RSTD:G_RSTD + 1]))
            self.vs(ve, ve.tensor_scalar_mul(
                out=self.stB[:], in0=self.stA[:],
                scalar1=self.gco[:, G_NEGMU:G_NEGMU + 1]))
            self.vs(ve, ve.reciprocal(out=self.stR[:], in_=self.st0[:]))

    def quant_tile(self, tt, src_ap, dstT, width=D):
        b = self.sblock
        @b.scalar
        def _(se):
            se.wait_ge(self.v.h, self.v.n)
            se.wait_ge(self.ld.h, self.ld.n)
            self.s.inc(se.activation(out=self.q_t1[:, 0:width], in_=src_ap,
                                     func=ACT.Identity,
                                     scale=self.stA[:, tt:tt + 1],
                                     bias=self.stB[:, tt:tt + 1]))
        @b.vector
        def _(ve):
            ve.wait_ge(self.s.h, self.s.n)
            ve.tensor_scalar(out=self.q_z[:, 0:width],
                             in0=self.q_t1[:, 0:width], scalar1=M32,
                             scalar2=-M32, op0=ALU.add, op1=ALU.add)
            self.vs(ve, ve.tensor_scalar(
                out=self.q_u[:, 0:width], in0=self.q_t1[:, 0:width],
                scalar1=M64, scalar2=-M64, op0=ALU.add, op1=ALU.add))
            self.v.inc(ve.scalar_tensor_tensor(
                out=self.q_r[:, 0:width], in0=self.q_z[:, 0:width],
                scalar=self.m1[:], in1=self.q_u[:, 0:width],
                op0=ALU.mult, op1=ALU.add))
        @b.scalar
        def _(se):
            se.wait_ge(self.v.h, self.v.n)
            se.wait_ge(self.t.h, self.t.n)
            self.s.inc(se.activation(out=self.q_xq[:, 0:width],
                                     in_=self.q_r[:, 0:width], func=ACT.Identity,
                                     scale=self.stR[:, tt:tt + 1]))
        self.transpose_to(
            self.q_xq, width // 128,
            lambda q, nb: dstT[:, q * 8:q * 8 + nb, tt * 128:(tt + 1) * 128])

    def preamble(self):
        b = self.sblock
        @b.sync
        def _(sy):
            for dst, src in [(self.midn, self.idn_d), (self.mcmask, self.cmask_d),
                             (self.gpvh, self.gpvh_d), (self.gpvl, self.gpvl_d),
                             (self.gpyh, self.gpyh_d), (self.gpyl, self.gpyl_d),
                             (self.csts, self.csts_d), (self.onesr, self.onesr_d),
                             (self.hones, self.hones_d), (self.prow, self.prm_d)]:
                self.ld.inc(sy.dma_start(out=dst[:], in_=src), 16)
        @b.vector
        def _(ve):
            ve.wait_ge(self.ld.h, self.ld.n)
            self.vs(ve, ve.tensor_scalar_add(
                out=self.st0[:, 0:1], in0=self.csts[:, C_MASK4:C_MASK4 + 1],
                scalar1=M64))
            self.vs(ve, ve.tensor_scalar_add(
                out=self.st1[:, 0:1], in0=self.st0[:, 0:1], scalar1=-M64))
            self.vs(ve, ve.tensor_scalar(
                out=self.m1[:], in0=self.st1[:, 0:1], scalar1=-0.25,
                scalar2=1.0, op0=ALU.mult, op1=ALU.add))
            self.vs(ve, ve.reciprocal(out=self.gco[:, G_R3M:G_R3M + 1],
                                      in_=self.csts[:, C_3M:C_3M + 1]))
            self.vs(ve, ve.reciprocal(out=self.gco[:, G_R1023:G_R1023 + 1],
                                      in_=self.csts[:, C_1023:C_1023 + 1]))
            self.vs(ve, ve.reciprocal(out=self.gco[:, G_R12:G_R12 + 1],
                                      in_=self.csts[:, C_12:C_12 + 1]))
        dsts = [self.aR, self.bR, self.aO, self.bO]
        for rnd in range(2):
            @b.tensor
            def _(te, rnd=rnd):
                self.vwait_all(te)
                for j in range(2):
                    jj = rnd * 2 + j
                    for c in range(0, D, 512):
                        self.t.inc(te.matmul(
                            self.pA[:, j * 1024 + c:j * 1024 + c + 512],
                            self.onesr[:],
                            self.prow[:, jj * D + c:jj * D + c + 512],
                            start=True, stop=True))
            @b.vector
            def _(ve, rnd=rnd):
                ve.wait_ge(self.t.h, self.t.n)
                for j in range(2):
                    jj = rnd * 2 + j
                    self.v.inc(ve.tensor_copy(
                        out=dsts[jj][:],
                        in_=self.pA[:, j * 1024:j * 1024 + 1024]))
        self.barrier()

    def weight_quant(self):
        for name, (O, Dd) in [("wgate", (INNER, D)), ("wup", (INNER, D)),
                              ("wdown", (D, INNER)), ("wq", (D, D)),
                              ("wk", (D, D)), ("wv", (D, D)), ("wg", (D, D)),
                              ("wor", (D, D)), ("wqkv", (3 * D, D)),
                              ("woo", (D, D))]:
            self._quant_one_weight(name, O, Dd)

    def _quant_one_weight(self, name, O, Dd):
        b = self.sblock
        w_d = self.w_in[name]
        not_ = O // 128
        ncw = Dd // 512
        ntile = not_ * ncw
        cnt = O * Dd
        for i in range(ntile):
            ot, cw = divmod(i, ncw)
            @b.sync
            def _(sy, i=i, ot=ot, cw=cw):
                self.ld.inc(sy.dma_start(
                    out=self.xt[i % 2][:, 0:512],
                    in_=w_d[ot * 128:(ot + 1) * 128,
                            cw * 512:(cw + 1) * 512]), 16)
            @b.vector
            def _(ve, i=i):
                self.v.inc(ve.tensor_reduce(
                    out=self.scr[:, i:i + 1], in_=self.xt[i % 2][:, 0:512],
                    axis=AX, op=ALU.add, apply_absolute_value=True))
        self.barrier()
        @b.vector
        def _(ve):
            self.vs(ve, ve.tensor_reduce(out=self.st0[:, 0:1],
                                         in_=self.scr[:, 0:ntile], axis=AX,
                                         op=ALU.add))
        @b.tensor
        def _(te):
            te.wait_ge(self.v.h, self.v.n)
            self.t.inc(te.matmul(self.pD[0:1, 4:5], self.st0[:, 0:1],
                                 self.csts[:, C_ONE:C_ONE + 1],
                                 start=True, stop=True))
        @b.vector
        def _(ve):
            ve.wait_ge(self.t.h, self.t.n)
            self.vs(ve, ve.tensor_copy(out=self.t11[:, 0:1],
                                       in_=self.pD[0:1, 4:5]))
            if cnt == 3 * (2 ** 20):
                self.vs(ve, ve.tensor_scalar_mul(
                    out=self.t11[:, 1:2], in0=self.t11[:, 0:1],
                    scalar1=self.gco[0:1, G_R3M:G_R3M + 1]))
            else:
                self.vs(ve, ve.tensor_scalar_mul(
                    out=self.t11[:, 1:2], in0=self.t11[:, 0:1],
                    scalar1=1.0 / cnt))
            self.vs(ve, ve.tensor_scalar_max(out=self.t11[:, 2:3],
                                             in0=self.t11[:, 1:2],
                                             scalar1=1e-5))
            self.vs(ve, ve.reciprocal(out=self.t11[:, 3:4],
                                      in_=self.t11[:, 2:3]))
        @b.tensor
        def _(te):
            te.wait_ge(self.v.h, self.v.n)
            self.t.inc(te.matmul(self.pD[:, 0:1], self.onesr[:],
                                 self.t11[:, 3:4], start=True, stop=True))
            self.t.inc(te.matmul(self.pD[:, 1:2], self.onesr[:],
                                 self.t11[:, 2:3], start=True, stop=True))
        @b.vector
        def _(ve):
            ve.wait_ge(self.t.h, self.t.n)
            self.vs(ve, ve.tensor_copy(out=self.gco[:, G_WS:G_WS + 2],
                                       in_=self.pD[:, 0:2]))
        self.barrier()
        base_ld = self.ld.n
        vb, ev, gtick = {}, {}, {}
        for i in range(ntile):
            ot, cw = divmod(i, ncw)
            @b.sync
            def _(sy, i=i, ot=ot, cw=cw):
                if i >= 2:
                    sy.wait_ge(self.v.h, vb[i - 2])
                self.ld.inc(sy.dma_start(
                    out=self.xt[i % 2][:, 0:512],
                    in_=w_d[ot * 128:(ot + 1) * 128,
                            cw * 512:(cw + 1) * 512]), 16)
            @b.vector
            def _(ve, i=i):
                ve.wait_ge(self.ld.h, base_ld + 16 * (i + 1))
                ve.wait_ge(self.t.h, self.t.n)
                self.vs(ve, ve.tensor_scalar_mul(
                    out=self.q_t1[:, 0:512], in0=self.xt[i % 2][:, 0:512],
                    scalar1=self.gco[:, G_WS:G_WS + 1]))
                self.vs(ve, ve.tensor_scalar(
                    out=self.q_r[:, 0:512], in0=self.q_t1[:, 0:512],
                    scalar1=1.0, scalar2=-1.0, op0=ALU.min, op1=ALU.max))
                ve.tensor_scalar(out=self.q_z[:, 0:512],
                                 in0=self.q_r[:, 0:512], scalar1=M32,
                                 scalar2=-M32, op0=ALU.add, op1=ALU.add)
                self.vs(ve, ve.tensor_scalar(
                    out=self.q_u[:, 0:512], in0=self.q_r[:, 0:512],
                    scalar1=M64, scalar2=-M64, op0=ALU.add, op1=ALU.add))
                self.vs(ve, ve.scalar_tensor_tensor(
                    out=self.q_t1[:, 512:1024], in0=self.q_z[:, 0:512],
                    scalar=self.m1[:], in1=self.q_u[:, 0:512],
                    op0=ALU.mult, op1=ALU.add))
                self.v.inc(ve.tensor_scalar_mul(
                    out=self.q_xq[:, 0:512], in0=self.q_t1[:, 512:1024],
                    scalar1=self.gco[:, G_C:G_C + 1]))
                vb[i] = self.v.n
            @b.tensor
            def _(te, i=i):
                te.wait_ge(self.v.h, vb[i])
                for j in range(4):
                    self.t.inc(te.transpose(
                        self.pT[:, j * 128:(j + 1) * 128],
                        self.q_xq[:, j * 128:(j + 1) * 128], self.midn[:]))
            @b.vector
            def _(ve, i=i):
                ve.wait_ge(self.t.h, self.t.n)
                if i >= 2:
                    ve.wait_ge(self.qg.h, gtick[i - 2])
                self.v.inc(ve.tensor_copy(
                    out=self.scrb[:, (i % 2) * 512:(i % 2) * 512 + 512],
                    in_=self.pT[:, 0:512]))
                ev[i] = self.v.n
            @b.gpsimd
            def _(ge, i=i, ot=ot, cw=cw):
                ge.wait_ge(self.v.h, ev[i])
                for j in range(4):
                    self.qg.inc(ge.dma_start(
                        out=self.wT[name][cw * 512 + j * 128:
                                          cw * 512 + (j + 1) * 128,
                                          ot * 128:(ot + 1) * 128],
                        in_=self.scrb[:, (i % 2) * 512 + j * 128:
                                      (i % 2) * 512 + (j + 1) * 128]), 16)
                gtick[i] = self.qg.n
        self.barrier()

    def p0_and_ffn(self):
        nc = self.nc
        b = self.sblock
        with contextlib.ExitStack() as es:
            E = es.enter_context
            xqfT = E(nc.sbuf_tensor("xqfT", [128, ND, N], BF16))
            for tt in range(NT):
                self._p0_tile(tt, xqfT)
            self.barrier()
            if self.upto >= 3:
                self._ffn_body(es, xqfT)
            self.barrier()

    def _p0_tile(self, tt, xqfT):
        b = self.sblock
        @b.sync
        def _(sy):
            self.vwait_all(sy)
            self.ld.inc(sy.dma_start(
                out=self.xt[0][:], in_=self.x_d[tt * 128:(tt + 1) * 128, :]), 16)
        @b.vector
        def _(ve):
            ve.wait_ge(self.ld.h, self.ld.n)
            self.vs(ve, ve.reduce_sum(out=self.st0[:, 0:1], in_=self.xt[0][:],
                                      axis=AX))
        @b.scalar
        def _(se):
            se.wait_ge(self.ld.h, self.ld.n)
            self.s.inc(se.activation(out=self.scr[:, 0:D], in_=self.xt[0][:],
                                     func=ACT.Square,
                                     accum_out=self.st0[:, 1:2]))
        @b.vector
        def _(ve):
            ve.wait_ge(self.s.h, self.s.n)
            self.vs(ve, ve.tensor_scalar_mul(out=self.st0[:, 2:3],
                                             in0=self.st0[:, 0:1],
                                             scalar1=1.0 / D))
            self.vs(ve, ve.tensor_mul(out=self.st0[:, 3:4],
                                      in0=self.st0[:, 2:3],
                                      in1=self.st0[:, 0:1]))
            self.vs(ve, ve.tensor_sub(out=self.st0[:, 4:5],
                                      in0=self.st0[:, 1:2],
                                      in1=self.st0[:, 3:4]))
            self.vs(ve, ve.tensor_scalar(out=self.st0[:, 5:6],
                                         in0=self.st0[:, 4:5],
                                         scalar1=1.0 / D, scalar2=1e-5,
                                         op0=ALU.mult, op1=ALU.add))
            self.vs(ve, ve.reciprocal(out=self.st0[:, 6:7],
                                      in_=self.st0[:, 5:6]))
        @b.scalar
        def _(se):
            se.wait_ge(self.v.h, self.v.n)
            self.s.inc(se.activation(out=self.st0[:, 7:8],
                                     in_=self.st0[:, 6:7], func=ACT.Sqrt))
        @b.vector
        def _(ve):
            ve.wait_ge(self.s.h, self.s.n)
            self.vs(ve, ve.scalar_tensor_tensor(
                out=self.st0[:, 8:9], in0=self.st0[:, 2:3], scalar=-1.0,
                in1=self.st0[:, 7:8], op0=ALU.mult, op1=ALU.mult))
            self.v.inc(ve.tensor_scalar(
                out=self.xt[1][:], in0=self.xt[0][:],
                scalar1=self.st0[:, 7:8], scalar2=self.st0[:, 8:9],
                op0=ALU.mult, op1=ALU.add))
            ve.wait_ge(self.v.h, self.v.n)
            self.v.inc(ve.reduce_sum(out=self.ssum[:, tt:tt + 1],
                                     in_=self.xt[1][:], axis=AX))
            self.v.inc(ve.tensor_reduce(out=self.smin[:, tt:tt + 1],
                                        in_=self.xt[1][:], axis=AX,
                                        op=ALU.min))
            self.v.inc(ve.reduce_max(out=self.smax[:, tt:tt + 1],
                                     in_=self.xt[1][:], axis=AX))
        @b.scalar
        def _(se):
            se.wait_ge(self.v.h, self.v.n)
            self.s.inc(se.activation(out=self.scr[:, 0:D], in_=self.xt[1][:],
                                     func=ACT.Square,
                                     accum_out=self.ssq[:, tt:tt + 1]))
        @b.gpsimd
        def _(ge):
            ge.wait_ge(self.v.h, self.v.n)
            self.qg.inc(ge.dma_start(
                out=self.xn_h[tt * 128:(tt + 1) * 128, :], in_=self.xt[1][:]), 16)
        @b.vector
        def _(ve):
            ve.wait_ge(self.v.h, self.v.n)
            self.vs(ve, ve.tensor_scalar_mul(out=self.st1[:, 0:1],
                                             in0=self.smin[:, tt:tt + 1],
                                             scalar1=-1.0))
            self.vs(ve, ve.tensor_max(out=self.st1[:, 1:2],
                                      in0=self.st1[:, 0:1],
                                      in1=self.smax[:, tt:tt + 1]))
            self.vs(ve, ve.tensor_scalar_max(out=self.st1[:, 2:3],
                                             in0=self.st1[:, 1:2],
                                             scalar1=1e-5))
            self.vs(ve, ve.reciprocal(out=self.st1[:, 3:4],
                                      in_=self.st1[:, 2:3]))
            self.vs(ve, ve.tensor_scalar_mul(out=self.stA[:, tt:tt + 1],
                                             in0=self.st1[:, 3:4],
                                             scalar1=127.0))
            self.vs(ve, ve.reciprocal(out=self.stR[:, tt:tt + 1],
                                      in_=self.stA[:, tt:tt + 1]))
            self.vs(ve, ve.memset(self.stB[:, tt:tt + 1], 0.0))
        self.quant_tile(tt, self.xt[1][:], xqfT)

    def _ffn_body(self, es, xqfT):
        nc = self.nc
        b = self.sblock
        E = es.enter_context
        wgb = E(nc.sbuf_tensor("wgb", [128, ND, 512], BF16))
        wub = E(nc.sbuf_tensor("wub", [128, ND, 512], BF16))
        wdb = [E(nc.sbuf_tensor(f"wdb{i}", [128, 512], BF16)) for i in range(2)]
        gate = E(nc.sbuf_tensor("gate", [128, 512], BF16))
        up = E(nc.sbuf_tensor("up", [128, 512], BF16))
        gu = E(nc.sbuf_tensor("gu", [128, 2, INNER], BF16))
        guqT = E(nc.sbuf_tensor("guqT", [128, 2, 32, 128], BF16))
        gq2 = E(nc.sbuf_tensor("gq2", [128, 2048], F32))
        gq3 = E(nc.sbuf_tensor("gq3", [128, 2048], F32))
        gqb = E(nc.sbuf_tensor("gqb", [128, INNER], BF16))
        for grp in range(8):
            tts = list(range(grp * 2, grp * 2 + 2))
            for ic in range(8):
                @b.sync
                def _(sy, ic=ic):
                    self.vwait_all(sy)
                    for dt in range(ND):
                        self.ld.inc(sy.dma_start(
                            out=wgb[:, dt, :],
                            in_=self.wT["wgate"][dt * 128:(dt + 1) * 128,
                                                 ic * 512:(ic + 1) * 512]), 16)
                        self.ld.inc(sy.dma_start(
                            out=wub[:, dt, :],
                            in_=self.wT["wup"][dt * 128:(dt + 1) * 128,
                                               ic * 512:(ic + 1) * 512]), 16)
                for ti, tt in enumerate(tts):
                    @b.tensor
                    def _(te, tt=tt, ti=ti, ic=ic):
                        te.wait_ge(self.ld.h, self.ld.n)
                        te.wait_ge(self.v.h, self.v.n)
                        te.wait_ge(self.s.h, self.s.n)
                        for dt in range(ND):
                            self.t.inc(te.matmul(
                                self.pA[:, 0:512],
                                xqfT[:, dt, tt * 128:(tt + 1) * 128],
                                wgb[:, dt, :], start=(dt == 0),
                                stop=(dt == ND - 1)))
                        for dt in range(ND):
                            self.t.inc(te.matmul(
                                self.pA[:, 512:1024],
                                xqfT[:, dt, tt * 128:(tt + 1) * 128],
                                wub[:, dt, :], start=(dt == 0),
                                stop=(dt == ND - 1)))
                    @b.scalar
                    def _(se, tt=tt, ti=ti, ic=ic):
                        se.wait_ge(self.t.h, self.t.n)
                        self.s.inc(se.activation(out=gate[:],
                                                 in_=self.pA[:, 0:512],
                                                 func=ACT.Silu))
                    @b.vector
                    def _(ve, tt=tt, ti=ti, ic=ic):
                        ve.wait_ge(self.t.h, self.t.n)
                        self.v.inc(ve.tensor_copy(out=up[:],
                                                  in_=self.pA[:, 512:1024]))
                        ve.wait_ge(self.s.h, self.s.n)
                        ve.wait_ge(self.v.h, self.v.n)
                        self.v.inc(ve.tensor_mul(
                            out=gu[:, ti, ic * 512:(ic + 1) * 512],
                            in0=gate[:], in1=up[:]))
            for ti, tt in enumerate(tts):
                @b.vector
                def _(ve, tt=tt, ti=ti):
                    self.vwait_all(ve)
                    self.vs(ve, ve.tensor_reduce(
                        out=self.st1[:, 0:1], in_=gu[:, ti, :], axis=AX,
                        op=ALU.max, apply_absolute_value=True))
                    self.vs(ve, ve.tensor_scalar_max(out=self.st1[:, 2:3],
                                                     in0=self.st1[:, 0:1],
                                                     scalar1=1e-5))
                    self.vs(ve, ve.reciprocal(out=self.st1[:, 3:4],
                                              in_=self.st1[:, 2:3]))
                    self.vs(ve, ve.tensor_scalar_mul(out=self.st1[:, 4:5],
                                                     in0=self.st1[:, 3:4],
                                                     scalar1=127.0))
                    self.vs(ve, ve.reciprocal(out=self.st1[:, 5:6],
                                              in_=self.st1[:, 4:5]))
                for hf in range(2):
                    @b.vector
                    def _(ve, tt=tt, ti=ti, hf=hf):
                        ve.wait_ge(self.s.h, self.s.n)
                        self.v.inc(ve.tensor_scalar_mul(
                            out=self.scr[:], in0=gu[:, ti, hf * 2048:
                                                    (hf + 1) * 2048],
                            scalar1=self.st1[:, 4:5]))
                        ve.wait_ge(self.v.h, self.v.n)
                        ve.tensor_scalar(out=gq2[:], in0=self.scr[:],
                                         scalar1=M32, scalar2=-M32,
                                         op0=ALU.add, op1=ALU.add)
                        self.vs(ve, ve.tensor_scalar(
                            out=gq3[:], in0=self.scr[:],
                            scalar1=M64, scalar2=-M64,
                            op0=ALU.add, op1=ALU.add))
                        self.v.inc(ve.scalar_tensor_tensor(
                            out=self.scr[:], in0=gq2[:], scalar=self.m1[:],
                            in1=gq3[:], op0=ALU.mult, op1=ALU.add))
                    @b.scalar
                    def _(se, tt=tt, ti=ti, hf=hf):
                        se.wait_ge(self.v.h, self.v.n)
                        self.s.inc(se.activation(
                            out=gqb[:, hf * 2048:(hf + 1) * 2048],
                            in_=self.scr[:], func=ACT.Identity,
                            scale=self.st1[:, 5:6]))
                self.transpose_to(
                    gqb, 32,
                    lambda q, nb, ti=ti: guqT[:, ti, q * 8:q * 8 + nb, :])
            for oc in range(2):
                for it in range(32):
                    @b.sync
                    def _(sy, oc=oc, it=it):
                        self.ld.inc(sy.dma_start(
                            out=wdb[it % 2][:],
                            in_=self.wT["wdown"][it * 128:(it + 1) * 128,
                                                 oc * 512:(oc + 1) * 512]), 16)
                    @b.tensor
                    def _(te, oc=oc, it=it):
                        for ti in range(2):
                            self.t.inc(te.matmul(
                                self.pA[:, ti * 512:(ti + 1) * 512],
                                guqT[:, ti, it, :], wdb[it % 2][:],
                                start=(it == 0), stop=(it == 31)))
                @b.scalar
                def _(se, oc=oc, grp=grp):
                    se.wait_ge(self.t.h, self.t.n)
                    se.wait_ge(self.qg.h, self.qg.n)
                    for ti in range(2):
                        self.s.inc(se.activation(
                            out=self.scr[:, ti * 512:(ti + 1) * 512],
                            in_=self.pA[:, ti * 512:(ti + 1) * 512],
                            func=ACT.Silu))
                @b.gpsimd
                def _(ge, oc=oc, grp=grp):
                    ge.wait_ge(self.s.h, self.s.n)
                    for ti in range(2):
                        tt = grp * 2 + ti
                        self.qg.inc(ge.dma_start(
                            out=self.g_h[tt * 128:(tt + 1) * 128,
                                         oc * 512:(oc + 1) * 512],
                            in_=self.scr[:, ti * 512:(ti + 1) * 512]), 16)

    def retention(self):
        nc = self.nc
        b = self.sblock
        with contextlib.ExitStack() as es:
            E = es.enter_context
            xnqT = E(nc.sbuf_tensor("xnqT", [128, ND, N], BF16))
            vsb = E(nc.sbuf_tensor("vsb", [128, NT, D], BF16))
            ytok = E(nc.sbuf_tensor("ytok", [128, NT, D], BF16))
            qsb = E(nc.sbuf_tensor("qsb", [128, N], BF16))
            ksb = E(nc.sbuf_tensor("ksb", [128, N], BF16))
            psb = E(nc.sbuf_tensor("psb", [128, 1024], BF16))
            yfm = E(nc.sbuf_tensor("yfm", [128, 512], BF16))
            wb1 = E(nc.sbuf_tensor("wb1", [128, ND, 128], BF16))
            wb2 = E(nc.sbuf_tensor("wb2", [128, ND, 512], BF16))
            self.bl_finalize(21)
            for tt in range(NT):
                @b.sync
                def _(sy, tt=tt):
                    self.vwait_all(sy)
                    self.ld.inc(sy.dma_start(
                        out=self.xt[0][:],
                        in_=self.xn_h[tt * 128:(tt + 1) * 128, :]), 16)
                self.quant_tile(tt, self.xt[0][:], xnqT)
            self.barrier()
            for oc in range(2):
                @b.sync
                def _(sy, oc=oc):
                    self.vwait_all(sy)
                    for dt in range(ND):
                        self.ld.inc(sy.dma_start(
                            out=wb2[:, dt, :],
                            in_=self.wT["wv"][dt * 128:(dt + 1) * 128,
                                              oc * 512:(oc + 1) * 512]), 16)
                for tt in range(NT):
                    @b.tensor
                    def _(te, tt=tt, oc=oc):
                        te.wait_ge(self.ld.h, self.ld.n)
                        te.wait_ge(self.v.h, self.v.n)
                        for dt in range(ND):
                            self.t.inc(te.matmul(
                                self.pC1[:],
                                xnqT[:, dt, tt * 128:(tt + 1) * 128],
                                wb2[:, dt, :], start=(dt == 0),
                                stop=(dt == ND - 1)))
                    @b.vector
                    def _(ve, tt=tt, oc=oc):
                        ve.wait_ge(self.t.h, self.t.n)
                        for hh in range(4):
                            h = oc * 4 + hh
                            col = tt * 8 + h
                            self.vs(ve, ve.tensor_scalar_mul(
                                out=self.scr[:, hh * 128:(hh + 1) * 128],
                                in0=self.pC1[:, hh * 128:(hh + 1) * 128],
                                scalar1=self.gpvh[:, col:col + 1]))
                            self.v.inc(ve.scalar_tensor_tensor(
                                out=vsb[:, tt, h * 128:(h + 1) * 128],
                                in0=self.pC1[:, hh * 128:(hh + 1) * 128],
                                scalar=self.gpvl[:, col:col + 1],
                                in1=self.scr[:, hh * 128:(hh + 1) * 128],
                                op0=ALU.mult, op1=ALU.add))
            self.barrier()
            for h in range(HR):
                for wnm, dst in (("wq", qsb), ("wk", ksb)):
                    @b.sync
                    def _(sy, wnm=wnm, h=h):
                        self.vwait_all(sy)
                        for dt in range(ND):
                            self.ld.inc(sy.dma_start(
                                out=wb1[:, dt, :],
                                in_=self.wT[wnm][dt * 128:(dt + 1) * 128,
                                                 h * 128:(h + 1) * 128]), 16)
                    @b.tensor
                    def _(te, h=h):
                        te.wait_ge(self.ld.h, self.ld.n)
                        te.wait_ge(self.v.h, self.v.n)
                        for dt in range(ND):
                            for c in range(4):
                                self.t.inc(te.matmul(
                                    self.pA[:, c * 512:(c + 1) * 512],
                                    wb1[:, dt, :],
                                    xnqT[:, dt, c * 512:(c + 1) * 512],
                                    start=(dt == 0), stop=(dt == ND - 1)))
                    @b.vector
                    def _(ve, dst=dst):
                        ve.wait_ge(self.t.h, self.t.n)
                        self.v.inc(ve.tensor_copy(out=dst[:], in_=self.pA[:]))
                for ncc in range(4):
                    nmt = 4 * ncc + 4
                    for rd in range(0, nmt, 2):
                        nb = min(2, nmt - rd)
                        @b.tensor
                        def _(te, rd=rd, nb=nb, ncc=ncc):
                            for m4 in range(nb):
                                mt = rd + m4
                                self.t.inc(te.matmul(
                                    self.pA[:, m4 * 512:m4 * 512 + 512],
                                    ksb[:, mt * 128:(mt + 1) * 128],
                                    qsb[:, ncc * 512:(ncc + 1) * 512],
                                    start=True, stop=True))
                        @b.vector
                        def _(ve, rd=rd, nb=nb, ncc=ncc):
                            for m4 in range(nb):
                                mt = rd + m4
                                q = mt - 4 * ncc
                                lo = max(q, 0) * 128
                                src = self.pA[:, m4 * 512:m4 * 512 + 512]
                                if q >= 0:
                                    self.v.inc(ve.tensor_mul(
                                        out=psb[:, m4 * 512 + lo:
                                                m4 * 512 + lo + 128],
                                        in0=src[:, lo:lo + 128],
                                        in1=self.mcmask[:]))
                                    if lo + 128 < 512:
                                        self.v.inc(ve.tensor_copy(
                                            out=psb[:, m4 * 512 + lo + 128:
                                                    m4 * 512 + 512],
                                            in_=src[:, lo + 128:512]))
                                else:
                                    self.v.inc(ve.tensor_copy(
                                        out=psb[:, m4 * 512:m4 * 512 + 512],
                                        in_=src[:]))
                        @b.tensor
                        def _(te, rd=rd, nb=nb, ncc=ncc, h=h):
                            for m4 in range(nb):
                                mt = rd + m4
                                q = mt - 4 * ncc
                                lo = max(q, 0) * 128
                                self.t.inc(te.matmul(
                                    self.pC1[:, lo:512],
                                    vsb[:, mt, h * 128:(h + 1) * 128],
                                    psb[:, m4 * 512 + lo:m4 * 512 + 512],
                                    start=(mt == 0), stop=(mt == nmt - 1)))
                    @b.vector
                    def _(ve, ncc=ncc):
                        ve.wait_ge(self.t.h, self.t.n)
                        self.v.inc(ve.tensor_copy(out=yfm[:], in_=self.pC1[:]))
                    @b.tensor
                    def _(te, ncc=ncc):
                        te.wait_ge(self.v.h, self.v.n)
                        for j in range(4):
                            self.t.inc(te.transpose(
                                self.pT[:, j * 128:(j + 1) * 128],
                                yfm[:, j * 128:(j + 1) * 128], self.midn[:]))
                    @b.vector
                    def _(ve, ncc=ncc, h=h):
                        ve.wait_ge(self.t.h, self.t.n)
                        for j in range(4):
                            nt = ncc * 4 + j
                            col = nt * 8 + h
                            self.vs(ve, ve.tensor_scalar_mul(
                                out=self.scr[:, j * 128:(j + 1) * 128],
                                in0=self.pT[:, j * 128:(j + 1) * 128],
                                scalar1=self.gpyh[:, col:col + 1]))
                            self.v.inc(ve.scalar_tensor_tensor(
                                out=ytok[:, nt, h * 128:(h + 1) * 128],
                                in0=self.pT[:, j * 128:(j + 1) * 128],
                                scalar=self.gpyl[:, col:col + 1],
                                in1=self.scr[:, j * 128:(j + 1) * 128],
                                op0=ALU.mult, op1=ALU.add))
            self.barrier()
            if self.tap == "ry":
                self.tap_sbuf_nt(ytok)
            wgb2 = E(nc.sbuf_tensor("wgb2", [128, ND, D], BF16))
            @b.sync
            def _(sy):
                self.vwait_all(sy)
                for dt in range(ND):
                    self.ld.inc(sy.dma_start(
                        out=wgb2[:, dt, :],
                        in_=self.wT["wg"][dt * 128:(dt + 1) * 128, :]), 16)
            for tt in range(NT):
                for oc in range(2):
                    @b.tensor
                    def _(te, tt=tt, oc=oc):
                        te.wait_ge(self.ld.h, self.ld.n)
                        te.wait_ge(self.s.h, self.s.n)
                        for dt in range(ND):
                            self.t.inc(te.matmul(
                                self.pC1[:],
                                xnqT[:, dt, tt * 128:(tt + 1) * 128],
                                wgb2[:, dt, oc * 512:(oc + 1) * 512],
                                start=(dt == 0), stop=(dt == ND - 1)))
                    @b.scalar
                    def _(se, tt=tt, oc=oc):
                        se.wait_ge(self.t.h, self.t.n)
                        self.s.inc(se.activation(
                            out=self.scrb[:, 1024 + oc * 512:
                                          1024 + oc * 512 + 512],
                            in_=self.pC1[:], func=ACT.Silu))
                self._ret_dyn_tanh_zz(tt, ytok)
            self.barrier()
            if self.tap == "rzz":
                self.tap_sbuf_nt(ytok)
            self.bl_finalize(21)
            for tt in range(NT):
                self.quant_tile(tt, ytok[:, tt, :], xnqT)
            self.barrier()
            for oc in range(2):
                @b.sync
                def _(sy, oc=oc):
                    self.vwait_all(sy)
                    for dt in range(ND):
                        self.ld.inc(sy.dma_start(
                            out=wb2[:, dt, :],
                            in_=self.wT["wor"][dt * 128:(dt + 1) * 128,
                                               oc * 512:(oc + 1) * 512]), 16)
                for tt in range(NT):
                    @b.sync
                    def _(sy, tt=tt, oc=oc):
                        self.vwait_all(sy)
                        self.ld.inc(sy.dma_start(
                            out=self.xt[0][:, 0:512],
                            in_=self.xn_h[tt * 128:(tt + 1) * 128,
                                          oc * 512:(oc + 1) * 512]), 16)
                    @b.tensor
                    def _(te, tt=tt, oc=oc):
                        te.wait_ge(self.v.h, self.v.n)
                        te.wait_ge(self.ld.h, self.ld.n)
                        for dt in range(ND):
                            self.t.inc(te.matmul(
                                self.pC1[:],
                                xnqT[:, dt, tt * 128:(tt + 1) * 128],
                                wb2[:, dt, :], start=(dt == 0),
                                stop=(dt == ND - 1)))
                    @b.vector
                    def _(ve, tt=tt, oc=oc):
                        ve.wait_ge(self.t.h, self.t.n)
                        ve.wait_ge(self.ld.h, self.ld.n)
                        self.v.inc(ve.tensor_add(out=self.xt[1][:, 0:512],
                                                 in0=self.pC1[:],
                                                 in1=self.xt[0][:, 0:512]))
                    @b.gpsimd
                    def _(ge, tt=tt, oc=oc):
                        ge.wait_ge(self.v.h, self.v.n)
                        self.qg.inc(ge.dma_start(
                            out=self.z_h[0][tt * 128:(tt + 1) * 128,
                                            oc * 512:(oc + 1) * 512],
                            in_=self.xt[1][:, 0:512]), 16)
            self.barrier()

    def _ret_dyn_tanh_zz(self, tt, ytok):
        self.dyn_tanh_tile(ytok[:, tt, :], self.aR, self.bR, self.q_xq[:])
        b = self.sblock
        @b.vector
        def _(ve, tt=tt):
            ve.wait_ge(self.s.h, self.s.n)
            self.v.inc(ve.tensor_mul(
                out=ytok[:, tt, :], in0=self.q_xq[:],
                in1=self.scrb[:, 1024:2048]))
            ve.wait_ge(self.v.h, self.v.n)
            self.v.inc(ve.reduce_sum(out=self.ssum[:, tt:tt + 1],
                                     in_=ytok[:, tt, :], axis=AX))
            self.v.inc(ve.tensor_reduce(out=self.smin[:, tt:tt + 1],
                                        in_=ytok[:, tt, :], axis=AX,
                                        op=ALU.min))
            self.v.inc(ve.reduce_max(out=self.smax[:, tt:tt + 1],
                                     in_=ytok[:, tt, :], axis=AX))
        @b.scalar
        def _(se, tt=tt):
            se.wait_ge(self.v.h, self.v.n)
            self.s.inc(se.activation(out=self.scr[:, 0:D],
                                     in_=ytok[:, tt, :], func=ACT.Square,
                                     accum_out=self.ssq[:, tt:tt + 1]))

    def dyn_tanh_tile(self, src_ap, a_rep, b_rep, out_bf_ap):
        b = self.sblock
        @b.vector
        def _(ve):
            self.vwait_all(ve)
            self.vs(ve, ve.reduce_sum(out=self.st0[:, 0:1], in_=src_ap,
                                      axis=AX))
        @b.scalar
        def _(se):
            se.wait_ge(self.v.h, self.v.n)
            self.s.inc(se.activation(out=self.scr[:, 0:D], in_=src_ap,
                                     func=ACT.Square,
                                     accum_out=self.st0[:, 1:2]))
        @b.vector
        def _(ve):
            ve.wait_ge(self.s.h, self.s.n)
            self.vs(ve, ve.tensor_scalar_mul(out=self.st0[:, 2:3],
                                             in0=self.st0[:, 0:1],
                                             scalar1=1.0 / D))
            self.vs(ve, ve.tensor_mul(out=self.st0[:, 3:4],
                                      in0=self.st0[:, 2:3],
                                      in1=self.st0[:, 0:1]))
            self.vs(ve, ve.tensor_sub(out=self.st0[:, 4:5],
                                      in0=self.st0[:, 1:2],
                                      in1=self.st0[:, 3:4]))
            self.vs(ve, ve.tensor_scalar_mul(
                out=self.st0[:, 5:6], in0=self.st0[:, 4:5],
                scalar1=self.gco[:, G_R1023:G_R1023 + 1]))
        @b.scalar
        def _(se):
            se.wait_ge(self.v.h, self.v.n)
            self.s.inc(se.activation(out=self.st0[:, 6:7],
                                     in_=self.st0[:, 5:6], func=ACT.Sqrt))
        @b.vector
        def _(ve):
            ve.wait_ge(self.s.h, self.s.n)
            self.vs(ve, ve.tensor_scalar_add(out=self.st0[:, 7:8],
                                             in0=self.st0[:, 6:7],
                                             scalar1=1e-6))
            self.vs(ve, ve.reciprocal(out=self.st0[:, 8:9],
                                      in_=self.st0[:, 7:8]))
            self.vs(ve, ve.scalar_tensor_tensor(
                out=self.st0[:, 9:10], in0=self.st0[:, 2:3], scalar=-1.0,
                in1=self.st0[:, 8:9], op0=ALU.mult, op1=ALU.mult))
            self.v.inc(ve.tensor_scalar(
                out=self.q_t1[:], in0=src_ap,
                scalar1=self.st0[:, 8:9], scalar2=self.st0[:, 9:10],
                op0=ALU.mult, op1=ALU.add))
            ve.wait_ge(self.v.h, self.v.n)
            self.v.inc(ve.tensor_mul(out=self.q_z[:], in0=self.q_t1[:],
                                     in1=a_rep[:]))
            ve.wait_ge(self.v.h, self.v.n)
            self.v.inc(ve.tensor_add(out=self.q_u[:], in0=self.q_z[:],
                                     in1=b_rep[:]))
        @b.scalar
        def _(se):
            se.wait_ge(self.v.h, self.v.n)
            self.s.inc(se.activation(out=out_bf_ap, in_=self.q_u[:],
                                     func=ACT.Tanh))

    def ode_all(self, nstage):
        nc = self.nc
        b = self.sblock
        with contextlib.ExitStack() as es:
            E = es.enter_context
            T1 = E(nc.sbuf_tensor("T1", [128, NT, D], BF16))
            hy = E(nc.sbuf_tensor("hy", [128, NT, D], BF16))
            hqT = E(nc.sbuf_tensor("hqT", [128, ND, N], BF16))
            vsb2 = E(nc.sbuf_tensor("vsb2", [128, NT, 130], BF16))
            qsb2 = E(nc.sbuf_tensor("qsb2", [128, N], BF16))
            ksb2 = E(nc.sbuf_tensor("ksb2", [128, N], BF16))
            psb2 = E(nc.sbuf_tensor("psb2", [128, 2048], BF16))
            yfm2 = E(nc.sbuf_tensor("yfm2", [128, 512], BF16))
            wqb = E(nc.sbuf_tensor("wqb", [128, ND, 128], BF16))
            wkb = E(nc.sbuf_tensor("wkb", [128, ND, 128], BF16))
            wvb = E(nc.sbuf_tensor("wvb", [128, ND, 128], BF16))
            woob = E(nc.sbuf_tensor("woob", [128, ND, 512], BF16))
            kts = E(nc.sbuf_tensor("kts", [128, 3 * D], BF16))
            @b.vector
            def _(ve):
                self.vwait_all(ve)
                self.v.inc(ve.memset(vsb2[:, :, 64:65], 1.0))
                self.v.inc(ve.memset(vsb2[:, :, 129:130], 1.0))
            for st in range(nstage):
                self._ode_stage_body(st, T1, hy, hqT, vsb2, qsb2, ksb2,
                                     psb2, yfm2, wqb, wkb, wvb, woob, kts)

    def _ode_stage_body(self, st, T1, hy, hqT, vsb2, qsb2, ksb2, psb2,
                        yfm2, wqb, wkb, wvb, woob, kts):
        b = self.sblock
        slot = st % 4
        step = st // 4
        zi = step % 2
        c_in = [None, 0.25, 0.25, 0.5][slot]
        if self.tap == f"oT1_{st}":
            self.tap_sbuf_nt(T1)
        for tt in range(NT):
            @b.sync
            def _(sy, tt=tt):
                self.vwait_all(sy)
                self.ld.inc(sy.dma_start(
                    out=self.xt[0][:],
                    in_=self.z_h[zi][tt * 128:(tt + 1) * 128, :]), 16)
            if slot > 0:
                @b.vector
                def _(ve, tt=tt):
                    ve.wait_ge(self.ld.h, self.ld.n)
                    self.v.inc(ve.scalar_tensor_tensor(
                        out=self.xt[1][:], in0=T1[:, tt, :], scalar=c_in,
                        in1=self.xt[0][:], op0=ALU.mult, op1=ALU.add))
                src = self.xt[1][:]
            else:
                src = self.xt[0][:]
            if self.tap == f"oarg_{st}":
                @b.gpsimd
                def _(ge, tt=tt, src=src):
                    ge.wait_ge(self.v.h, self.v.n)
                    ge.wait_ge(self.ld.h, self.ld.n)
                    self.qg.inc(ge.dma_start(
                        out=self.tap_d[tt * 128:(tt + 1) * 128, :],
                        in_=src), 16)
                @b.vector
                def _(ve, tt=tt):
                    ve.wait_ge(self.qg.h, self.qg.n)
            self.dyn_tanh_tile(src, self.aO, self.bO, hy[:, tt, :])
            @b.vector
            def _(ve, tt=tt):
                ve.wait_ge(self.s.h, self.s.n)
                self.v.inc(ve.reduce_sum(out=self.ssum[:, tt:tt + 1],
                                         in_=hy[:, tt, :], axis=AX))
                self.v.inc(ve.tensor_reduce(out=self.smin[:, tt:tt + 1],
                                            in_=hy[:, tt, :], axis=AX,
                                            op=ALU.min))
                self.v.inc(ve.reduce_max(out=self.smax[:, tt:tt + 1],
                                         in_=hy[:, tt, :], axis=AX))
            @b.scalar
            def _(se, tt=tt):
                se.wait_ge(self.v.h, self.v.n)
                self.s.inc(se.activation(out=self.scr[:, 0:D],
                                         in_=hy[:, tt, :], func=ACT.Square,
                                         accum_out=self.ssq[:, tt:tt + 1]))
        self.barrier()
        if self.tap == f"oh1_{st}":
            self.tap_sbuf_nt(hy)
        self.bl_finalize(21)
        for tt in range(NT):
            self.quant_tile(tt, hy[:, tt, :], hqT)
        self.barrier()
        for hp in range(HP):
            @b.sync
            def _(sy, hp=hp):
                self.vwait_all(sy)
                for dt in range(ND):
                    self.ld.inc(sy.dma_start(
                        out=wqb[:, dt, :],
                        in_=self.wT["wqkv"][dt * 128:(dt + 1) * 128,
                                            hp * 128:(hp + 1) * 128]), 16)
                    self.ld.inc(sy.dma_start(
                        out=wkb[:, dt, :],
                        in_=self.wT["wqkv"][dt * 128:(dt + 1) * 128,
                                            D + hp * 128:D + (hp + 1) * 128]),
                        16)
                    self.ld.inc(sy.dma_start(
                        out=wvb[:, dt, :],
                        in_=self.wT["wqkv"][dt * 128:(dt + 1) * 128,
                                            2 * D + hp * 128:
                                            2 * D + (hp + 1) * 128]), 16)
            for tr in range(4):
                @b.tensor
                def _(te, tr=tr):
                    for ti in range(4):
                        tt = 4 * tr + ti
                        for dt in range(ND):
                            self.t.inc(te.matmul(
                                self.pA[:, ti * 512:ti * 512 + 128],
                                hqT[:, dt, tt * 128:(tt + 1) * 128],
                                wvb[:, dt, :], start=(dt == 0),
                                stop=(dt == ND - 1)))
                @b.vector
                def _(ve, tr=tr):
                    for ti in range(4):
                        tt = 4 * tr + ti
                        self.v.inc(ve.tensor_copy(
                            out=vsb2[:, tt, :].rearrange(
                                "p (g c) -> p g c", g=2)[:, :, 0:64],
                            in_=self.pA[:, ti * 512:ti * 512 + 128].rearrange(
                                "p (g c) -> p g c", g=2)))
            for wb, dst in ((wqb, qsb2), (wkb, ksb2)):
                @b.tensor
                def _(te, wb=wb):
                    te.wait_ge(self.v.h, self.v.n)
                    for dt in range(ND):
                        for c in range(4):
                            self.t.inc(te.matmul(
                                self.pA[:, c * 512:(c + 1) * 512],
                                wb[:, dt, :],
                                hqT[:, dt, c * 512:(c + 1) * 512],
                                start=(dt == 0), stop=(dt == ND - 1)))
                @b.vector
                def _(ve, dst=dst):
                    ve.wait_ge(self.t.h, self.t.n)
                    self.v.inc(ve.tensor_copy(out=dst[:], in_=self.pA[:]))
            for qi, qk in enumerate((qsb2, ksb2)):
                @b.scalar
                def _(se, qk=qk):
                    se.wait_ge(self.v.h, self.v.n)
                    self.s.inc(se.activation(out=self.scrb[:], in_=qk[:],
                                             func=ACT.Square))
                @b.vector
                def _(ve, qi=qi):
                    ve.wait_ge(self.s.h, self.s.n)
                    self.vs(ve, ve.reduce_max(out=self.st0[:, qi:qi + 1],
                                              in_=self.scrb[:], axis=AX))
            @b.tensor
            def _(te):
                te.wait_ge(self.v.h, self.v.n)
                self.t.inc(te.matmul(self.pD[0:1, 0:2], self.st0[:, 0:1],
                                     self.hones[:], start=True, stop=True))
                self.t.inc(te.matmul(self.pD[0:1, 2:4], self.st0[:, 1:2],
                                     self.hones[:], start=True, stop=True))
            @b.vector
            def _(ve):
                ve.wait_ge(self.t.h, self.t.n)
                self.vs(ve, ve.tensor_copy(out=self.t11[:, 0:4],
                                           in_=self.pD[0:1, 0:4]))
            @b.scalar
            def _(se):
                se.wait_ge(self.v.h, self.v.n)
                self.s.inc(se.activation(out=self.t11[:, 4:8],
                                         in_=self.t11[:, 0:4], func=ACT.Sqrt))
            @b.vector
            def _(ve):
                ve.wait_ge(self.s.h, self.s.n)
                self.vs(ve, ve.tensor_mul(out=self.t11[:, 8:10],
                                          in0=self.t11[:, 4:6],
                                          in1=self.t11[:, 6:8]))
                self.vs(ve, ve.tensor_scalar_mul(out=self.t11[:, 10:12],
                                                 in0=self.t11[:, 8:10],
                                                 scalar1=-0.125))
            @b.tensor
            def _(te):
                te.wait_ge(self.v.h, self.v.n)
                self.t.inc(te.matmul(self.pD[:, 0:2], self.onesr[:],
                                     self.t11[:, 10:12], start=True,
                                     stop=True))
            @b.vector
            def _(ve):
                ve.wait_ge(self.t.h, self.t.n)
                self.vs(ve, ve.tensor_copy(out=self.negCb[:],
                                           in_=self.pD[:, 0:2]))
            for h01 in range(2):
                bp = h01 * 64
                h = hp * 2 + h01
                for ncc in range(4):
                    def _attnv(te, r, h01=h01):
                        for m4 in range(4):
                            mt = 4 * r + m4
                            self.t.inc(te.matmul(
                                self.pC1[0:65, :],
                                vsb2[:, mt, h01 * 65:h01 * 65 + 65],
                                psb2[:, m4 * 512:(m4 + 1) * 512],
                                start=(mt == 0), stop=(mt == 15)))
                    for r in range(4):
                        @b.tensor
                        def _(te, r=r, ncc=ncc, bp=bp):
                            if r > 0:
                                _attnv(te, r - 1)
                            for m4 in range(4):
                                mt = 4 * r + m4
                                self.t.inc(te.matmul(
                                    self.pA[:, m4 * 512:(m4 + 1) * 512],
                                    ksb2[bp:bp + 64, mt * 128:(mt + 1) * 128],
                                    qsb2[bp:bp + 64,
                                         ncc * 512:(ncc + 1) * 512],
                                    start=True, stop=True))
                        @b.scalar
                        def _(se, r=r, h01=h01):
                            self.s.inc(se.activation(
                                out=psb2[:, 0:2048],
                                in_=self.pA[:, 0:2048],
                                func=ACT.Exp, scale=0.125,
                                bias=self.negCb[:, h01:h01 + 1]))
                    @b.tensor
                    def _(te):
                        _attnv(te, 3)
                    @b.vector
                    def _(ve, ncc=ncc):
                        ve.wait_ge(self.t.h, self.t.n)
                        self.v.inc(ve.tensor_copy(out=yfm2[0:65, :],
                                                  in_=self.pC1[0:65, :]))
                    @b.tensor
                    def _(te, ncc=ncc):
                        te.wait_ge(self.v.h, self.v.n)
                        for j in range(4):
                            self.t.inc(te.transpose(
                                self.pT[:, j * 128:j * 128 + 65],
                                yfm2[0:65, j * 128:(j + 1) * 128],
                                self.midn[0:65, 0:65]))
                    @b.vector
                    def _(ve, ncc=ncc, h=h):
                        ve.wait_ge(self.t.h, self.t.n)
                        for j in range(4):
                            ins = ve.reciprocal(out=self.st0[:, j:j + 1],
                                                in_=self.pT[:, j * 128 + 64:
                                                            j * 128 + 65])
                        self.v.inc(ins)
                        ve.wait_ge(self.v.h, self.v.n)
                        for j in range(4):
                            nt = ncc * 4 + j
                            self.v.inc(ve.tensor_scalar_mul(
                                out=hy[:, nt, h * 64:(h + 1) * 64],
                                in0=self.pT[:, j * 128:j * 128 + 64],
                                scalar1=self.st0[:, j:j + 1]))
        self.barrier()
        if self.tap == f"oy_{st}":
            self.tap_sbuf_nt(hy)
        for tb in range(4):
            @b.vector
            def _(ve, tb=tb):
                for ti in range(4):
                    tt = 4 * tb + ti
                    self.v.inc(ve.reduce_sum(out=self.ssum[:, tt:tt + 1],
                                             in_=hy[:, tt, :], axis=AX))
                    self.v.inc(ve.tensor_reduce(out=self.smin[:, tt:tt + 1],
                                                in_=hy[:, tt, :], axis=AX,
                                                op=ALU.min))
                    self.v.inc(ve.reduce_max(out=self.smax[:, tt:tt + 1],
                                             in_=hy[:, tt, :], axis=AX))
            @b.scalar
            def _(se, tb=tb):
                for ti in range(4):
                    tt = 4 * tb + ti
                    self.s.inc(se.activation(out=self.scr[:, 0:D],
                                             in_=hy[:, tt, :],
                                             func=ACT.Square,
                                             accum_out=self.ssq[:, tt:tt + 1]))
        self.barrier()
        self.bl_finalize(21)
        if self.tap == f"ost_{st}":
            @b.gpsimd
            def _(ge):
                self.vwait_all(ge)
                for j, ten in enumerate([self.stA, self.stB, self.stR,
                                         self.smin, self.smax, self.ssum,
                                         self.ssq, self.gco]):
                    self.qg.inc(ge.dma_start(
                        out=self.tap_d[0:128, j * 16:(j + 1) * 16],
                        in_=ten[:, 0:16]), 16)
            self.barrier()
        for tt in range(NT):
            self.quant_tile(tt, hy[:, tt, :], hqT)
        self.barrier()
        for oc in range(2):
            @b.sync
            def _(sy, oc=oc):
                self.vwait_all(sy)
                for dt in range(ND):
                    self.ld.inc(sy.dma_start(
                        out=woob[:, dt, :],
                        in_=self.wT["woo"][dt * 128:(dt + 1) * 128,
                                           oc * 512:(oc + 1) * 512]), 16)
            for tp in range(NT // 2):
                @b.tensor
                def _(te, tp=tp, oc=oc):
                    for ti, pj in ((0, self.pC1), (1, self.pC2)):
                        tt = 2 * tp + ti
                        for dt in range(ND):
                            self.t.inc(te.matmul(
                                pj[:], hqT[:, dt, tt * 128:(tt + 1) * 128],
                                woob[:, dt, :],
                                start=(dt == 0), stop=(dt == ND - 1)))
                @b.vector
                def _(ve, tp=tp, oc=oc):
                    for ti, pj in ((0, self.pC1), (1, self.pC2)):
                        tt = 2 * tp + ti
                        self.v.inc(ve.tensor_copy(
                            out=T1[:, tt, oc * 512:(oc + 1) * 512],
                            in_=pj[:]))
                if slot < 3 and oc == 1:
                    @b.gpsimd
                    def _(ge, tp=tp):
                        for ti in range(2):
                            tt = 2 * tp + ti
                            self.qg.inc(ge.dma_start(
                                out=self.k_h[slot][tt * 128:(tt + 1) * 128, :],
                                in_=T1[:, tt, :]), 16)
        self.barrier()
        if slot == 3:
            for tt in range(NT):
                @b.sync
                def _(sy, tt=tt):
                    self.vwait_all(sy)
                    self.ld.inc(sy.dma_start(
                        out=self.xt[0][:],
                        in_=self.z_h[zi][tt * 128:(tt + 1) * 128, :]), 16)
                    for j in range(3):
                        self.ld.inc(sy.dma_start(
                            out=kts[:, j * D:(j + 1) * D],
                            in_=self.k_h[j][tt * 128:(tt + 1) * 128, :]), 16)
                @b.vector
                def _(ve, tt=tt):
                    ve.wait_ge(self.ld.h, self.ld.n)
                    self.v.inc(ve.tensor_add(out=self.q_t1[:],
                                             in0=kts[:, D:2 * D],
                                             in1=kts[:, 2 * D:3 * D]))
                    ve.wait_ge(self.v.h, self.v.n)
                    self.v.inc(ve.scalar_tensor_tensor(
                        out=self.q_z[:], in0=self.q_t1[:], scalar=2.0,
                        in1=kts[:, 0:D], op0=ALU.mult, op1=ALU.add))
                    ve.wait_ge(self.v.h, self.v.n)
                    self.v.inc(ve.tensor_add(out=self.q_u[:],
                                             in0=self.q_z[:],
                                             in1=T1[:, tt, :]))
                    ve.wait_ge(self.v.h, self.v.n)
                    self.v.inc(ve.tensor_scalar(
                        out=self.q_r[:], in0=self.q_u[:],
                        scalar1=self.gco[:, G_R12:G_R12 + 1],
                        scalar2=None, op0=ALU.mult))
                    ve.wait_ge(self.v.h, self.v.n)
                    self.v.inc(ve.tensor_add(out=self.xt[1][:],
                                             in0=self.q_r[:],
                                             in1=self.xt[0][:]))
                @b.gpsimd
                def _(ge, tt=tt):
                    ge.wait_ge(self.v.h, self.v.n)
                    self.qg.inc(ge.dma_start(
                        out=self.z_h[1 - zi][tt * 128:(tt + 1) * 128, :],
                        in_=self.xt[1][:]), 16)
            self.barrier()

    def final_combine(self):
        b = self.sblock
        for tt in range(NT):
            @b.sync
            def _(sy, tt=tt):
                self.vwait_all(sy)
                self.ld.inc(sy.dma_start(
                    out=self.xt[0][:],
                    in_=self.z_h[0][tt * 128:(tt + 1) * 128, :]), 16)
                self.ld.inc(sy.dma_start(
                    out=self.xt[1][:],
                    in_=self.g_h[tt * 128:(tt + 1) * 128, :]), 16)
                self.ld.inc(sy.dma_start(
                    out=self.q_t1[:],
                    in_=self.x_d[tt * 128:(tt + 1) * 128, :]), 16)
            @b.vector
            def _(ve, tt=tt):
                ve.wait_ge(self.ld.h, self.ld.n)
                self.v.inc(ve.tensor_mul(out=self.q_z[:], in0=self.xt[0][:],
                                         in1=self.xt[1][:]))
                ve.wait_ge(self.v.h, self.v.n)
                self.v.inc(ve.tensor_add(out=self.q_u[:], in0=self.q_z[:],
                                         in1=self.q_t1[:]))
            @b.gpsimd
            def _(ge, tt=tt):
                ge.wait_ge(self.v.h, self.v.n)
                self.qg.inc(ge.dma_start(
                    out=self.out_d[tt * 128:(tt + 1) * 128, :],
                    in_=self.q_u[:]), 16)
        @b.gpsimd
        def _(ge):
            ge.wait_ge(self.qg.h, self.qg.n)


    def tap_sbuf_nt(self, ten):
        b = self.sblock
        self.barrier()
        for tt in range(NT):
            @b.vector
            def _(ve, tt=tt):
                self.vwait_all(ve)
                self.v.inc(ve.tensor_copy(out=self.xt[0][:],
                                          in_=ten[:, tt, :]))
            @b.gpsimd
            def _(ge, tt=tt):
                ge.wait_ge(self.v.h, self.v.n)
                self.qg.inc(ge.dma_start(
                    out=self.tap_d[tt * 128:(tt + 1) * 128, :],
                    in_=self.xt[0][:]), 16)
        self.barrier()

    def emit_tap(self):
        if self.tap is None:
            return
        srcs = {"xn": self.xn_h, "g": self.g_h, "z0": self.z_h[0],
                "z1": self.z_h[1], "k0": self.k_h[0], "k1": self.k_h[1],
                "k2": self.k_h[2]}
        if self.tap not in srcs:
            return
        src = srcs[self.tap]
        isbf = self.tap.startswith("k")
        b = self.sblock
        self.barrier()
        for tt in range(NT):
            @b.sync
            def _(sy, tt=tt):
                self.vwait_all(sy)
                self.ld.inc(sy.dma_start(
                    out=self.scrb[:, 0:D] if isbf else self.xt[0][:],
                    in_=src[tt * 128:(tt + 1) * 128, :]), 16)
            if isbf:
                @b.vector
                def _(ve, tt=tt):
                    ve.wait_ge(self.ld.h, self.ld.n)
                    self.v.inc(ve.tensor_copy(out=self.xt[0][:],
                                              in_=self.scrb[:, 0:D]))
            @b.gpsimd
            def _(ge, tt=tt):
                ge.wait_ge(self.ld.h, self.ld.n)
                ge.wait_ge(self.v.h, self.v.n)
                self.qg.inc(ge.dma_start(
                    out=self.tap_d[tt * 128:(tt + 1) * 128, :],
                    in_=self.xt[0][:]), 16)
        self.barrier()


def _host_consts():
    g64 = 1.0 - np.exp(np.linspace(math.log(1 / 32), math.log(1 / 512), HR))
    gpv = np.empty((128, 128), np.float64)
    gpy = np.empty((128, 128), np.float64)
    for mt in range(NT):
        for h in range(HR):
            m = (mt * 128 + np.arange(128)).astype(np.float64)
            gpv[:, mt * 8 + h] = g64[h] ** (-m)
            gpy[:, mt * 8 + h] = (g64[h] ** m) * (2.0 ** -3.5)
    def hilo(a):
        hi = a.astype(np.float32)
        lo = (a - hi.astype(np.float64)).astype(np.float32)
        return hi, lo
    gpvh, gpvl = hilo(gpv)
    gpyh, gpyl = hilo(gpy)
    idn = np.eye(128, dtype=ml_dtypes.bfloat16)
    cm = (np.arange(128)[None, :] >= np.arange(128)[:, None]) \
        .astype(ml_dtypes.bfloat16)
    csts = np.zeros((128, 8), np.float32)
    csts[:, C_MASK4] = 4.0
    csts[:, C_3M] = 3.0 * 2 ** 20
    csts[:, C_1023] = 1023.0
    csts[:, C_12] = 12.0
    csts[:, C_ONE] = 1.0
    onesr = np.ones((1, 128), np.float32)
    hones = np.zeros((128, 2), np.float32)
    hones[0:64, 0] = 1.0
    hones[64:128, 1] = 1.0
    return dict(idn=idn, cmask=cm, gpvh=gpvh, gpvl=gpvl, gpyh=gpyh,
                gpyl=gpyl, csts=csts, onesr=onesr, hones=hones)


def make_in_maps(inputs):
    consts = _host_consts()
    prm = np.concatenate([
        np.asarray(inputs["L0_attn_a_r"], np.float32).reshape(1, D),
        np.asarray(inputs["L0_attn_b_r"], np.float32).reshape(1, D),
        np.asarray(inputs["L0_attn_a_o"], np.float32).reshape(1, D),
        np.asarray(inputs["L0_attn_b_o"], np.float32).reshape(1, D)], axis=1)
    w = {
        "wq": np.asarray(inputs["L0_attn_Wq"], np.float32),
        "wk": np.asarray(inputs["L0_attn_Wk"], np.float32),
        "wv": np.asarray(inputs["L0_attn_Wv"], np.float32),
        "wg": np.asarray(inputs["L0_attn_Wg"], np.float32),
        "wor": np.asarray(inputs["L0_attn_Wo_r"], np.float32),
        "wqkv": np.asarray(inputs["L0_attn_Wqkv"], np.float32),
        "woo": np.asarray(inputs["L0_attn_Wo_o"], np.float32),
        "wgate": np.asarray(inputs["L0_ffn_w_gate"], np.float32),
        "wup": np.asarray(inputs["L0_ffn_w_up"], np.float32),
        "wdown": np.asarray(inputs["L0_ffn_w_down"], np.float32),
    }
    x = np.asarray(inputs["L0_attn_x"], np.float32)
    in_maps = []
    for c in range(NCORE):
        m = {"x": np.ascontiguousarray(x[c]), "prm": prm}
        m.update(w)
        m.update(consts)
        in_maps.append(m)
    return in_maps


def kernel(**inputs):
    in_maps = make_in_maps(inputs)
    bld = Builder()
    nc = bld.build()
    res = run_bass_kernel_spmd(nc, in_maps, core_ids=list(range(NCORE)))
    return np.stack([res.results[c]["out"] for c in range(NCORE)], axis=0)
